# version 09 (final)
# speedup vs baseline: 1.6833x; 1.0165x over previous
import numpy as np
import ml_dtypes
import concourse.bass as bass
import concourse.mybir as mybir
from concourse.bass_utils import run_bass_kernel_spmd
from contextlib import ExitStack

F32 = mybir.dt.float32
BF16 = mybir.dt.bfloat16
AF = mybir.ActivationFunctionType
ALU = mybir.AluOpType
AX = mybir.AxisListType.X
BF = ml_dtypes.bfloat16

D = 2048
TOK = 2048
TC = 512
NTC = TOK // TC
NDM = D // 128


ENGINES = ("gpsimd", "sync", "tensor", "scalar", "vector")


class Sem:
    __slots__ = ("h", "name", "n")

    def __init__(self, handle, name):
        self.h = handle
        self.name = name
        self.n = 0


class OpRec:
    __slots__ = ("eng", "meth", "args", "kwargs", "incs")

    def __init__(self, eng, meth, args, kwargs):
        self.eng = eng
        self.meth = meth
        self.args = args
        self.kwargs = kwargs
        self.incs = []

    def inc(self, sem: Sem, n: int = 1):
        sem.n += n
        self.incs.append((sem, n, sem.n))
        return self


class EngProxy:
    def __init__(self, sched, eng):
        self._sched = sched
        self._eng = eng

    def wait(self, sem: Sem, val: int | None = None):
        v = sem.n if val is None else val
        self._sched.ops[self._eng].append(("wait", sem, v))

    def __getattr__(self, meth):
        if meth.startswith("_"):
            raise AttributeError(meth)
        sched = self._sched
        eng = self._eng

        def call(*args, **kwargs):
            rec = OpRec(eng, meth, args, kwargs)
            sched.ops[eng].append(("op", rec))
            sched.n_ops += 1
            return rec
        return call


class Sched:

    def __init__(self, nc):
        self.nc = nc
        self.ctx = ExitStack()
        self.ops = {e: [] for e in ENGINES}
        self.n_ops = 0
        self._semcnt = 0
        self.g = EngProxy(self, "gpsimd")
        self.y = EngProxy(self, "sync")
        self.t = EngProxy(self, "tensor")
        self.s = EngProxy(self, "scalar")
        self.v = EngProxy(self, "vector")

    def sem(self, name=None):
        self._semcnt += 1
        h = self.ctx.enter_context(self.nc.semaphore(name or f"sem{self._semcnt}"))
        sm = Sem(h, name)
        if not hasattr(self, "_all_sems"):
            self._all_sems = []
        self._all_sems.append(sm)
        return sm

    def barrier(self):
        snap = [(sm, sm.n) for sm in getattr(self, "_all_sems", [])]
        for e in ENGINES:
            self.ops[e].append(("barrier", snap))

    def sbuf(self, name, shape, dtype):
        return self.ctx.enter_context(self.nc.sbuf_tensor(name, list(shape), dtype))

    def psum(self, name, shape, dtype=F32):
        return self.ctx.enter_context(self.nc.psum_tensor(name, list(shape), dtype))

    _CHAIN_ENGINES = ("vector", "scalar", "gpsimd")
    _NOCHAIN_METHS = {"dma_start", "indirect_dma_start", "collective_compute",
                      "nop", "dma_start_transpose"}

    def realize(self):
        nc = self.nc
        chain_sems = {e: self.sem(f"chain_{e}") for e in self._CHAIN_ENGINES}
        chain_at_barrier = {e: [] for e in self._CHAIN_ENGINES}
        for e in self._CHAIN_ENGINES:
            cc = 0
            for item in self.ops[e]:
                if item[0] == "barrier":
                    chain_at_barrier[e].append(cc)
                elif item[0] == "op":
                    rec = item[1]
                    if rec.meth not in self._NOCHAIN_METHS and not rec.incs:
                        cc += 1
        block = self.ctx.enter_context(nc.Block())
        engobjs = {
            "gpsimd": nc.gpsimd, "sync": nc.sync, "tensor": nc.tensor,
            "scalar": nc.scalar, "vector": nc.vector,
        }

        def make_body(eng):
            def body(_e):
                obj = engobjs[eng]
                chain = chain_sems.get(eng)
                ccount = 0
                prev_signal = None
                bar_i = 0
                for item in self.ops[eng]:
                    if item[0] == "wait":
                        _, sem, v = item
                        if v > 0:
                            obj.wait_ge(sem.h, v)
                    elif item[0] == "barrier":
                        snap = item[1]
                        for sm, n in snap:
                            if n > 0:
                                obj.wait_ge(sm.h, n)
                        for ce in self._CHAIN_ENGINES:
                            cv = chain_at_barrier[ce][bar_i]
                            if cv > 0 and ce != eng:
                                obj.wait_ge(chain_sems[ce].h, cv)
                        bar_i += 1
                    else:
                        rec = item[1]
                        assert len(rec.incs) <= 1, f"multiple incs on {rec.meth}"
                        do_chain = chain is not None and rec.meth not in self._NOCHAIN_METHS
                        if do_chain and prev_signal is not None:
                            obj.wait_ge(*prev_signal)
                        inst = getattr(obj, rec.meth)(*rec.args, **rec.kwargs)
                        if rec.incs:
                            sem, n, after = rec.incs[0]
                            inst = inst.then_inc(sem.h) if n == 1 else inst.then_inc(sem.h, n)
                            if do_chain:
                                prev_signal = (sem.h, after)
                        elif do_chain:
                            inst = inst.then_inc(chain.h)
                            ccount += 1
                            prev_signal = (chain.h, ccount)
            return body

        for eng in ENGINES:
            if self.ops[eng]:
                getattr(block, eng)(make_body(eng))
        self.ctx.close()
        return nc



def dma_fm(S_engine, dst_tile, src_ap, nchunk, width):
    return S_engine.dma_start(
        out=dst_tile[:].rearrange("p (c n) -> p c n", c=nchunk),
        in_=src_ap.rearrange("(c p) n -> p c n", p=128))

def ts_slice(t, i, w):
    return t[:, i * w:(i + 1) * w]



def host_consts():
    idn = np.eye(128, dtype=BF)
    ones_col = np.ones((128, 1), dtype=BF)
    ones_row = np.ones((1, 128), dtype=BF)
    ones_row_f = np.ones((1, 128), dtype=np.float32)
    sel = np.zeros((8, 8 * 128), dtype=BF)
    for e in range(8):
        sel[e, e * 128:(e + 1) * 128] = 1.0
    return {"c_idn": idn, "c_ones_col": ones_col, "c_ones_row": ones_row,
            "c_ones_row_f": ones_row_f, "c_sel": sel}


def declare_consts(nc):
    return {
        "idn": nc.dram_tensor("c_idn", [128, 128], BF16, kind="ExternalInput").ap(),
        "ones_col": nc.dram_tensor("c_ones_col", [128, 1], BF16, kind="ExternalInput").ap(),
        "ones_row": nc.dram_tensor("c_ones_row", [1, 128], BF16, kind="ExternalInput").ap(),
        "ones_row_f": nc.dram_tensor("c_ones_row_f", [1, 128], F32, kind="ExternalInput").ap(),
        "sel": nc.dram_tensor("c_sel", [8, 8 * 128], BF16, kind="ExternalInput").ap(),
    }


class Common:

    def __init__(self, S: Sched, consts):
        self.S = S
        self.ones_col = S.sbuf("cc_ones_col", (128, 1), BF16)
        self.ones_row = S.sbuf("cc_ones_row", (1, 128), BF16)
        self.ones_row_f = S.sbuf("cc_ones_row_f", (1, 128), F32)
        self.sel = S.sbuf("cc_sel", (8, 8 * 128), BF16)
        self.idn = S.sbuf("cc_idn", (128, 128), BF16)
        ld = S.sem("cc_ld")
        g = S.g
        g.dma_start(out=self.ones_col[:], in_=consts["ones_col"]).inc(ld, 16)
        g.dma_start(out=self.ones_row[:], in_=consts["ones_row"]).inc(ld, 16)
        g.dma_start(out=self.ones_row_f[:], in_=consts["ones_row_f"]).inc(ld, 16)
        g.dma_start(out=self.sel[:], in_=consts["sel"]).inc(ld, 16)
        g.dma_start(out=self.idn[:], in_=consts["idn"]).inc(ld, 16)
        S.t.wait(ld)
        S.v.wait(ld)
        S.s.wait(ld)

    def sel_e(self, e):
        return self.sel[:, e * 128:(e + 1) * 128]




class SharedTiles:

    def __init__(self, S: Sched):
        self.x = S.sbuf("sh_x", (128, NDM * TC), F32)
        self.zx = S.sbuf("sh_zx", (128, NDM * TC), F32)
        self.hbuf = S.sbuf("sh_h", (128, NDM * TC), BF16)
        self.xhi = S.sbuf("sh_xhi", (128, NDM * TC), BF16)
        self.scr = S.sbuf("sh_scr", (128, 16384), BF16)
        self.w4 = [S.sbuf(f"sh_w4{i}", (128, 4096), BF16) for i in range(2)]
        self.sdl = [S.sbuf(f"sh_sdl{i}", (128, 1024), BF16) for i in range(2)]
        self.ce = S.sbuf("sh_ce", (128, 8 * TC), BF16)
        self.fones = S.sbuf("sh_ones", (128, TC), BF16)
        self.f = [S.sbuf(f"sh_f{i}", (128, TC), F32) for i in range(6)]
        self.b = [S.sbuf(f"sh_b{i}", (128, TC), BF16) for i in range(4)]
        self.e = [S.sbuf(f"sh_e{i}", (128, 8), F32) for i in range(3)]
        self.eb0 = S.sbuf("sh_eb0", (128, 8), BF16)
        self.ct8 = S.sbuf("sh_ct8", (8, 2 * TC), BF16)
        self.c = [S.sbuf(f"sh_c{i}", (128, 1), F32) for i in range(4)]
        self.wc = S.sbuf("sh_wc", (128, 48), F32)
        self.mk = S.sbuf("sh_mk", (128, 8), F32)
        self.kt = [S.sbuf(f"sh_kt{i}", (128, 32), F32) for i in range(4)]
        self.kr = [S.sbuf(f"sh_kr{i}", (128, 32), F32) for i in range(4)]
        self.rw = S.sbuf("sh_rw", (128, NDM * 8), BF16)
        self.row = [S.sbuf(f"sh_row{i}", (1, TC), F32) for i in range(4)]
        self.t0h = S.sbuf("sh_t0h", (128, 5 * TC), F32)
        self.t0b = S.sbuf("sh_t0b", (128, TC), F32)
        self.mmask = S.sbuf("sh_mmk", (128, 4 * TC), BF16)
        self.offtab = S.sbuf("sh_oft", (128, 64), F32)
        self.qkvb = S.sbuf("sh_qkvb", (128, 12), F32)
        self.outb = S.sbuf("sh_outb", (128, 16), F32)
        self.slop = S.sbuf("sh_slop", (1, 4), F32)
        self.noff = S.sbuf("sh_noff", (1, 64), F32)
        self.selm = S.sbuf("sh_selm", (128, 4), F32)
        self.r1x1 = [S.sbuf(f"sh_r1x{i}", (1, 1), F32) for i in range(2)]
        self.p_s1 = S.psum("shp_s1", (128, TC))
        self.p_s2 = S.psum("shp_s2", (128, TC))
        self.p_mu = S.psum("shp_mu", (128, TC))
        self.p_rstd = S.psum("shp_rstd", (128, TC))
        self.pA = S.psum("shp_A", (128, TC))
        self.pB = S.psum("shp_B", (128, TC))
        self.pC = S.psum("shp_C", (128, TC))
        self.pD = S.psum("shp_D", (128, TC), BF16)
        for nm in ["ld", "cld", "mm", "ev_v", "ev_s", "st", "wl0", "wl1",
                   "wol0", "wol1", "sl0", "sl1", "ss0", "ss1", "dnl", "rwl",
                   "gst", "agc", "gld", "rhl", "qkl", "hst", "qst", "ast"]:
            setattr(self, nm, S.sem("s_" + nm))

    def w4s(self, col, width):
        tnsr = self.w4[col // 4096]
        c = col % 4096
        assert c + width <= 4096
        return tnsr[:, c:c + width]



def emit_ln_stats(S, C, src, xhi, sq, p_s1, p_s2, mu_row, rstd_row,
                  p_mu, p_rstd, mm, ev_v, ev_s, eps=1e-5):
    v, s, t = S.v, S.s, S.t
    v.tensor_copy(xhi[:], src[:]).inc(ev_v)
    s.activation(sq[:], src[:], AF.Square).inc(ev_s)
    t.wait(ev_v)
    t.wait(ev_s)
    for dmc in range(NDM):
        t.matmul(p_s1[0:1, :], C.ones_col[:], ts_slice(xhi, dmc, TC),
                 start=(dmc == 0), stop=(dmc == NDM - 1))
    for dmc in range(NDM):
        op = t.matmul(p_s2[0:1, :], C.ones_col[:], ts_slice(sq, dmc, TC),
                      start=(dmc == 0), stop=(dmc == NDM - 1))
    op.inc(mm)
    v.wait(mm)
    v.tensor_scalar_mul(mu_row[:], p_s1[0:1, :], 1.0 / D).inc(ev_v)
    s.wait(ev_v)
    s.activation(rstd_row[:], mu_row[:], AF.Square).inc(ev_s)
    v.wait(ev_s)
    v.scalar_tensor_tensor(rstd_row[:], p_s2[0:1, :], 1.0 / D, rstd_row[:],
                           ALU.mult, ALU.subtract)
    v.tensor_scalar_add(rstd_row[:], rstd_row[:], eps).inc(ev_v)
    s.wait(ev_v)
    s.activation(rstd_row[:], rstd_row[:], AF.Sqrt).inc(ev_s)
    v.wait(ev_s)
    v.reciprocal(rstd_row[:], rstd_row[:]).inc(ev_v)
    t.wait(ev_v)
    t.matmul(p_mu[:], C.ones_row_f[:], mu_row[:], start=True, stop=True)
    t.matmul(p_rstd[:], C.ones_row_f[:], rstd_row[:], start=True, stop=True).inc(mm)



def emit_moe(S, C, T, xT, yT, rw, gate_w, up_w, down_w, scale=0.5):
    g, y, t, s, v = S.g, S.y, S.t, S.s, S.v

    acc = [T.x, T.zx]
    rw_sb = T.rw
    mu_row = T.row[0]
    rstd_row = T.row[1]
    tmp = T.f[0]
    exrow = T.e[0]
    prow = T.e[1]
    keepr = T.e[2]
    m1c = T.c[0]
    m2c = T.c[1]
    dnc = T.c[2]
    crow = T.eb0
    cT2 = T.ct8
    sq = T.hbuf
    cast = T.xhi

    p_s1 = T.p_s1
    p_s2 = T.p_s2
    p_mu = T.p_mu
    p_rstd = T.p_rstd
    p_ct = T.pD
    pA = T.pA

    ld = T.ld
    wld_r = T.rwl
    wldgu = [T.wl0, T.wl1]
    wld_d = T.dnl
    mm = T.mm
    ev_v = T.ev_v
    ev_s = T.ev_s
    st = T.st

    def h_sl(tch, dmc):
        i = (tch * NDM + dmc) * TC
        return T.scr[:, i:i + TC]

    def act_sl(fj, tch):
        i = (fj * 2 + tch) * TC
        return T.xhi[:, i:i + TC]

    g.dma_start(out=rw_sb[:], in_=rw[:]).inc(wld_r, 16)

    pair_idx = 0
    pair_evv = []
    pair_evs = []
    pairs_done = []
    dexp_evv = 0
    dn_mm = 0
    for half in range(2):
        for tch in range(2):
            tc = half * 2 + tch
            y.wait(st)
            y.wait(ev_v)
            y.dma_start(out=acc[tch][:], in_=xT[tc]).inc(ld, 16)
            v.wait(ld)
            s.wait(ld)
            x_tc = acc[tch]
            v.tensor_copy(cast[:], x_tc[:]).inc(ev_v)
            s.activation(sq[:], x_tc[:], AF.Square).inc(ev_s)
            t.wait(ev_v)
            t.wait(ev_s)
            for dmc in range(NDM):
                t.matmul(p_s1[0:1, :], C.ones_col[:], ts_slice(cast, dmc, TC),
                         start=(dmc == 0), stop=(dmc == NDM - 1))
            for dmc in range(NDM):
                op = t.matmul(p_s2[0:1, :], C.ones_col[:], ts_slice(sq, dmc, TC),
                              start=(dmc == 0), stop=(dmc == NDM - 1))
            op.inc(mm)
            v.wait(mm)
            v.tensor_scalar_mul(mu_row[:], p_s1[0:1, :], 1.0 / D).inc(ev_v)
            s.wait(ev_v)
            s.activation(rstd_row[:], mu_row[:], AF.Square).inc(ev_s)
            v.wait(ev_s)
            v.scalar_tensor_tensor(rstd_row[:], p_s2[0:1, :], 1.0 / D, rstd_row[:],
                                   ALU.mult, ALU.subtract)
            v.tensor_scalar_add(rstd_row[:], rstd_row[:], 1e-5).inc(ev_v)
            s.wait(ev_v)
            s.activation(rstd_row[:], rstd_row[:], AF.Sqrt).inc(ev_s)
            v.wait(ev_s)
            v.reciprocal(rstd_row[:], rstd_row[:]).inc(ev_v)
            t.wait(ev_v)
            t.matmul(p_mu[:], C.ones_row_f[:], mu_row[:], start=True, stop=True)
            t.matmul(p_rstd[:], C.ones_row_f[:], rstd_row[:], start=True, stop=True) \
                .inc(mm)
            v.wait(mm)
            for dmc in range(NDM):
                v.tensor_sub(tmp[:], ts_slice(x_tc, dmc, TC), p_mu[:])
                op = v.tensor_mul(h_sl(tch, dmc), tmp[:], p_rstd[:])
            op.inc(ev_v)
            t.wait(ev_v)
            t.wait(wld_r, 16)
            for tt in range(TC // 128):
                for dmc in range(NDM):
                    op = t.matmul(pA[:, 0:8],
                                  T.scr[:, (tch * NDM + dmc) * TC + tt * 128:
                                        (tch * NDM + dmc) * TC + tt * 128 + 128],
                                  ts_slice(rw_sb, dmc, 8),
                                  start=(dmc == 0), stop=(dmc == NDM - 1))
                op.inc(mm)
                v.wait(mm)
                s.wait(mm)
                v.reduce_max(m1c[:], pA[:, 0:8], axis=AX, negate=True).inc(ev_v)
                s.wait(ev_v)
                s.activation(exrow[:], pA[:, 0:8], AF.Exp, bias=m1c[:]).inc(ev_s)
                v.wait(ev_s)
                v.reduce_sum(m2c[:], exrow[:], axis=AX)
                v.reciprocal(m2c[:], m2c[:])
                v.tensor_scalar_mul(prow[:], exrow[:], m2c[:])
                v.reduce_max(m1c[:], prow[:], axis=AX)
                v.tensor_scalar(keepr[:], prow[:], m1c[:], None, ALU.is_ge)
                v.scalar_tensor_tensor(exrow[:], keepr[:], -1e9, prow[:],
                                       ALU.mult, ALU.add)
                v.reduce_max(m2c[:], exrow[:], axis=AX)
                v.tensor_scalar(keepr[:], prow[:], m2c[:], None, ALU.is_ge)
                v.tensor_add(dnc[:], m1c[:], m2c[:])
                v.reciprocal(dnc[:], dnc[:])
                v.tensor_mul(prow[:], prow[:], keepr[:])
                v.tensor_scalar(crow[:], prow[:], dnc[:], float(scale),
                                ALU.mult, ALU.mult).inc(ev_v)
                t.wait(ev_v)
                t.wait(ev_s)
                t.transpose(p_ct[0:8, tt * 128:(tt + 1) * 128],
                            crow[:], C.idn[:]).inc(mm)
            v.wait(mm)
            v.tensor_copy(cT2[:, tch * TC:(tch + 1) * TC], p_ct[0:8, 0:TC]).inc(ev_v)

        for e in range(8):
            t.wait(ev_v)
            t.matmul(pA[:], C.sel_e(e), cT2[:, 0:TC], start=True, stop=True)
            t.matmul(p_s1[:], C.sel_e(e), cT2[:, TC:2 * TC], start=True, stop=True) \
                .inc(mm)
            p_ce = [pA, p_s1]
            for fj in range(8):
                buf = pair_idx % 2
                if pair_idx >= 2:
                    g.wait(ev_v, pairs_done[pair_idx - 2])
                g.wait(mm, dn_mm)
                g.dma_start(out=T.w4[buf][:, 0:D], in_=gate_w[e, fj]).inc(wldgu[buf], 16)
                g.dma_start(out=T.w4[buf][:, D:2 * D], in_=up_w[e, fj]).inc(wldgu[buf], 16)
                t.wait(wldgu[buf], wldgu[buf].n)
                for tch in range(2):
                    par = (fj * 2 + tch) % 2
                    pg = [T.pB, T.p_mu][par]
                    pu = [T.pC, T.p_rstd][par]
                    sg2 = [T.b[0], T.b[2]][par]
                    su2 = [T.b[1], T.b[3]][par]
                    u = len(pair_evs)
                    if u >= 2:
                        t.wait(ev_s, pair_evs[u - 2])
                        t.wait(ev_v, pair_evv[u - 2])
                    for dmc in range(NDM):
                        t.matmul(pg[:], T.w4[buf][:, dmc * 128:dmc * 128 + 128],
                                 h_sl(tch, dmc),
                                 start=(dmc == 0), stop=(dmc == NDM - 1))
                    for dmc in range(NDM):
                        op = t.matmul(pu[:], T.w4[buf][:, D + dmc * 128:D + dmc * 128 + 128],
                                      h_sl(tch, dmc),
                                      start=(dmc == 0), stop=(dmc == NDM - 1))
                    op.inc(mm)
                    s.wait(mm)
                    if u >= 2:
                        s.wait(ev_v, pair_evv[u - 2])
                    s.activation(sg2[:], pg[:], AF.Silu).inc(ev_s)
                    pair_evs.append(ev_s.n)
                    v.wait(mm)
                    v.tensor_copy(su2[:], pu[:])
                    v.wait(ev_s)
                    v.tensor_mul(act_sl(fj, tch), sg2[:], su2[:])
                    v.tensor_mul(act_sl(fj, tch), act_sl(fj, tch), p_ce[tch][:]) \
                        .inc(ev_v)
                    pair_evv.append(ev_v.n)
                pairs_done_cur = ev_v.n
                pair_idx += 1
                pairs_done.append(pairs_done_cur)
            g.wait(ev_v, dexp_evv if e == 0 else pair_evv[-1])
            dwb = wld_d.n
            for fj in range(4):
                g.dma_start(out=ts_slice(T.hbuf, fj, D),
                            in_=down_w[e, fj * 128:(fj + 1) * 128, :]).inc(wld_d, 16)
            g.wait(ev_s, pair_evs[-1])
            for fj in range(4, 8):
                g.dma_start(out=T.w4s((fj - 4) * D, D),
                            in_=down_w[e, fj * 128:(fj + 1) * 128, :]).inc(wld_d, 16)
            t.wait(wld_d, wld_d.n)
            t.wait(ev_v)
            down_evv = []
            for dmo in range(NDM):
                for tch in range(2):
                    un = dmo * 2 + tch
                    pd = [T.pB, T.pC, T.p_mu, T.p_rstd][un % 4]
                    if un >= 4:
                        t.wait(ev_v, down_evv[un - 4])
                    for fj in range(8):
                        lhs = (T.hbuf[:, fj * D + dmo * 128:fj * D + dmo * 128 + 128]
                               if fj < 4 else
                               T.w4s((fj - 4) * D + dmo * 128, 128))
                        op = t.matmul(pd[:], lhs, act_sl(fj, tch),
                                      start=(fj == 0), stop=(fj == 7))
                    op.inc(mm)
                    v.wait(mm)
                    v.tensor_add(ts_slice(acc[tch], dmo, TC),
                                 ts_slice(acc[tch], dmo, TC), pd[:]).inc(ev_v)
                    down_evv.append(ev_v.n)
            dexp_evv = ev_v.n
            dn_mm = mm.n
        for tch in range(2):
            tc = half * 2 + tch
            y.wait(ev_v)
            y.dma_start(out=yT[tc], in_=acc[tch][:]).inc(st, 16)
    y.wait(st)


def emit_griffin(S, C, T, xT, yT, wa_w, wi_w, wv_w, wout_w, wgu_w, wdn_w,
                 wcols, masks, s_dram, d_dram, ag_in, ag_out):
    g, y, t, s, v = S.g, S.y, S.t, S.s, S.v
    CH = 64

    x_tc = T.x
    zx = T.zx
    hbuf = T.hbuf
    xhi = T.xhi
    sq = hbuf
    scr = T.scr
    wgu = T.w4
    wo = T.w4
    sdl = T.sdl
    af = T.f[1]
    if_ = T.f[2]
    vf = T.f[5]
    t1f = T.f[0]
    cdt = T.f[3]
    srf = T.f[5]
    drf = T.f[4]
    onesf = T.fones
    carry = T.kt[0]
    dcarry = T.kt[1]
    st0 = T.kt[2]
    t3 = T.kt[3]
    agsb = T.kt[3]
    rtile = T.kr
    wc_sb = T.wc
    mk_sb = T.mk
    mu_row = T.row[0]
    rstd_row = T.row[1]
    rr_row = T.row[2]
    t2_row = T.row[3]

    p_s1 = T.p_s1
    p_s2 = T.p_s2
    p_mu = T.p_mu
    p_rstd = T.p_rstd
    pA = T.pA
    pB = T.pB
    pC = T.pC

    ld = T.ld
    cld = T.cld
    mm = T.mm
    ev_v = T.ev_v
    ev_s = T.ev_s
    st = T.st
    wl = [T.wl0, T.wl1]
    wol = [T.wol0, T.wol1]
    sdls = [T.sl0, T.sl1]
    sdst = [T.ss0, T.ss1]
    dnl = T.dnl
    agst = T.gst
    agc = T.agc
    agld = T.gld

    g.dma_start(out=wc_sb[:], in_=wcols).inc(cld, 16)
    g.dma_start(out=mk_sb[:], in_=masks).inc(cld, 16)
    v.wait(cld)
    s.wait(cld)
    t.wait(cld)
    v.memset(onesf[:], 1.0)
    v.memset(carry[:], 0.0)
    v.memset(dcarry[:], 1.0).inc(ev_v)
    wpre_c = lambda k: wc_sb[:, k:k + 1]
    wffn_c = lambda k: wc_sb[:, 16 + k:17 + k]
    db_c = lambda k: wc_sb[:, 32 + k:33 + k]

    def ln_rows(tc, save, src=None):
        src = x_tc if src is None else src
        v.tensor_copy(xhi[:], src[:]).inc(ev_v)
        s.activation(sq[:], src[:], AF.Square).inc(ev_s)
        t.wait(ev_v)
        t.wait(ev_s)
        for dmc in range(NDM):
            t.matmul(p_s1[0:1, :], C.ones_col[:], ts_slice(xhi, dmc, TC),
                     start=(dmc == 0), stop=(dmc == NDM - 1))
        for dmc in range(NDM):
            op = t.matmul(p_s2[0:1, :], C.ones_col[:], ts_slice(sq, dmc, TC),
                          start=(dmc == 0), stop=(dmc == NDM - 1))
        op.inc(mm)
        v.wait(mm)
        v.tensor_scalar_mul(mu_row[:], p_s1[0:1, :], 1.0 / D).inc(ev_v)
        s.wait(ev_v)
        s.activation(rstd_row[:], mu_row[:], AF.Square).inc(ev_s)
        v.wait(ev_s)
        v.scalar_tensor_tensor(t2_row[:], p_s2[0:1, :], 1.0 / D, rstd_row[:],
                               ALU.mult, ALU.subtract)
        v.tensor_scalar_add(rstd_row[:], t2_row[:], 1e-5).inc(ev_v)
        s.wait(ev_v)
        s.activation(rstd_row[:], rstd_row[:], AF.Sqrt).inc(ev_s)
        v.wait(ev_s)
        v.reciprocal(rstd_row[:], rstd_row[:]).inc(ev_v)
        t.wait(ev_v)
        t.matmul(p_mu[:], C.ones_row_f[:], mu_row[:], start=True, stop=True)
        t.matmul(p_rstd[:], C.ones_row_f[:], rstd_row[:], start=True, stop=True).inc(mm)

    slot = 0
    slot_mm = []
    aiv_evs = []
    aiv_evv = []
    sdst_n = [0, 0]
    for tc in range(NTC):
        y.wait(st)
        y.wait(ev_v)
        y.dma_start(out=x_tc[:], in_=xT[tc]).inc(ld, 16)
        v.wait(ld)
        s.wait(ld)
        ln_rows(tc, save=True)
        v.wait(mm)
        v.tensor_mul(rr_row[:], t2_row[:], rstd_row[:])
        v.tensor_mul(rr_row[:], rr_row[:], rstd_row[:])
        v.tensor_scalar_add(rr_row[:], rr_row[:], 1e-6).inc(ev_v)
        s.wait(ev_v)
        s.activation(rr_row[:], rr_row[:], AF.Sqrt).inc(ev_s)
        v.wait(ev_s)
        v.reciprocal(rr_row[:], rr_row[:])
        v.tensor_mul(rr_row[:], rr_row[:], rstd_row[:]).inc(ev_v)
        t.wait(ev_v)
        t.matmul(p_rstd[:], C.ones_row_f[:], rr_row[:], start=True, stop=True).inc(mm)
        v.wait(mm)
        for dmc in range(NDM):
            v.tensor_sub(t1f[:], ts_slice(x_tc, dmc, TC), p_mu[:])
            v.tensor_mul(t1f[:], t1f[:], p_rstd[:])
            op = v.tensor_scalar_mul(ts_slice(hbuf, dmc, TC), t1f[:], wpre_c(dmc))
        op.inc(ev_v)
        h_done = ev_v.n

        for k in range(NDM):
            par = slot % 2
            base = par * 3
            pset = [(pA, pB, pC), (p_mu, p_rstd, p_s2)][par]
            if k < 2:
                t.wait(ev_v, h_done)
            if slot >= 2:
                g.wait(mm, slot_mm[slot - 2])
            g.dma_start(out=ts_slice(scr, base + 0, D), in_=wa_w[k]).inc(wl[par], 16)
            g.dma_start(out=ts_slice(scr, base + 1, D), in_=wi_w[k]).inc(wl[par], 16)
            g.dma_start(out=ts_slice(scr, base + 2, D), in_=wv_w[k]).inc(wl[par], 16)
            t.wait(wl[par], wl[par].n)
            if slot >= 2:
                t.wait(ev_s, aiv_evs[slot - 2])
                t.wait(ev_v, aiv_evv[slot - 2])
            for dmc in range(NDM):
                t.matmul(pset[0][:], scr[:, (base + 0) * D + dmc * 128:(base + 0) * D + dmc * 128 + 128],
                         ts_slice(hbuf, dmc, TC), start=(dmc == 0), stop=(dmc == NDM - 1))
            for dmc in range(NDM):
                t.matmul(pset[1][:], scr[:, (base + 1) * D + dmc * 128:(base + 1) * D + dmc * 128 + 128],
                         ts_slice(hbuf, dmc, TC), start=(dmc == 0), stop=(dmc == NDM - 1))
            for dmc in range(NDM):
                op = t.matmul(pset[2][:], scr[:, (base + 2) * D + dmc * 128:(base + 2) * D + dmc * 128 + 128],
                              ts_slice(hbuf, dmc, TC), start=(dmc == 0), stop=(dmc == NDM - 1))
            op.inc(mm)
            slot_mm.append(mm.n)
            s.wait(mm)
            s.wait(ev_v)
            v.wait(mm)
            s.activation(af[:], pset[0][:], AF.Sigmoid, bias=db_c(k))
            s.activation(if_[:], pset[1][:], AF.Sigmoid).inc(ev_s)
            aiv_evs.append(ev_s.n)
            v.tensor_copy(vf[:], pset[2][:]).inc(ev_v)
            aiv_evv.append(ev_v.n)
            s.activation(t1f[:], af[:], AF.Square).inc(ev_s)
            v.wait(ev_s)
            v.tensor_scalar(t1f[:], t1f[:], -1.0, 1.0, ALU.mult, ALU.add)
            v.tensor_scalar_max(t1f[:], t1f[:], 1e-8).inc(ev_v)
            s.wait(ev_v)
            s.activation(t1f[:], t1f[:], AF.Sqrt).inc(ev_s)
            v.wait(ev_s)
            v.tensor_mul(if_[:], t1f[:], if_[:])
            v.tensor_mul(if_[:], if_[:], vf[:])
            v.tensor_scalar_max(t1f[:], af[:], 1e-10)
            for ch in range(TC // CH):
                v.tensor_tensor_scan(cdt[:, ch * CH:(ch + 1) * CH],
                                     t1f[:, ch * CH:(ch + 1) * CH],
                                     onesf[:, ch * CH:(ch + 1) * CH],
                                     1.0, ALU.mult, ALU.mult)
            v.tensor_scalar(cdt[:], cdt[:], 1e10, 1.0, ALU.mult, ALU.min)
            v.tensor_mul(cdt[:], cdt[:], if_[:])
            if tc == 0:
                v.tensor_tensor_scan(srf[:], af[:], cdt[:], 0.0, ALU.mult, ALU.add)
                v.tensor_tensor_scan(drf[:], af[:], onesf[:], 1.0, ALU.mult, ALU.mult)
            else:
                v.tensor_tensor_scan(srf[:], af[:], cdt[:], carry[:, k:k + 1],
                                     ALU.mult, ALU.add)
                v.tensor_tensor_scan(drf[:], af[:], onesf[:], dcarry[:, k:k + 1],
                                     ALU.mult, ALU.mult)
            v.tensor_copy(carry[:, k:k + 1], srf[:, TC - 1:TC])
            v.tensor_copy(dcarry[:, k:k + 1], drf[:, TC - 1:TC])
            v.wait(sdst[par], sdst_n[par])
            v.tensor_copy(sdl[par][:, 0:TC], srf[:])
            v.tensor_copy(sdl[par][:, TC:2 * TC], drf[:]).inc(ev_v)
            y.wait(ev_v)
            y.dma_start(out=s_dram[tc, k], in_=sdl[par][:, 0:TC]).inc(sdst[par], 16)
            y.dma_start(out=d_dram[tc, k], in_=sdl[par][:, TC:2 * TC]).inc(sdst[par], 16)
            sdst_n[par] = sdst[par].n
            slot += 1

    v.tensor_copy(agsb[:, 0:16], dcarry[:, 0:16])
    v.tensor_copy(agsb[:, 16:32], carry[:, 0:16]).inc(ev_v)
    g.wait(ev_v)
    g.dma_start(out=ag_in, in_=agsb[:]).inc(agst, 16)
    g.wait(agst, 16)
    g.collective_compute(
        "AllGather", ALU.bypass,
        replica_groups=[[0, 1, 2, 3], [4, 5, 6, 7]],
        ins=[ag_in], outs=[ag_out],
    ).inc(agc)
    g.wait(agc, 1)
    for r in range(4):
        g.dma_start(out=rtile[r][:], in_=ag_out[r * 128:(r + 1) * 128, :]).inc(agld, 16)
    v.wait(agld, 64)
    v.memset(st0[:, 0:16], 0.0)
    for r in range(4):
        v.tensor_mul(t3[:, 0:16], rtile[r][:, 0:16], st0[:, 0:16])
        v.tensor_add(t3[:, 0:16], t3[:, 0:16], rtile[r][:, 16:32])
        v.tensor_scalar_mul(t3[:, 0:16], t3[:, 0:16], mk_sb[:, r:r + 1])
        v.tensor_scalar_mul(st0[:, 0:16], st0[:, 0:16], mk_sb[:, 4 + r:5 + r])
        op = v.tensor_add(st0[:, 0:16], st0[:, 0:16], t3[:, 0:16])
    op.inc(ev_v)

    sd_slot = 0
    sd_evv = []
    wo_slot = 0
    wo_mm = []
    wo_evv = []
    gu_n = 0
    gu_evs = []
    gu_evv = []
    gu_done = []
    dn_mm2 = 0
    acc = [x_tc, zx]

    def z_sl(tch, dmc):
        i = (tch * NDM + dmc) * TC
        return scr[:, i:i + TC]

    def act5_sl(fjj, tch):
        i = (fjj * 2 + tch) * TC
        return xhi[:, i:i + TC]

    y.wait(sdst[0], sdst[0].n)
    y.wait(sdst[1], sdst[1].n)
    for half in range(2):
        for tch in range(2):
            tc = half * 2 + tch
            y.wait(st)
            y.wait(ev_v)
            y.dma_start(out=acc[tch][:], in_=xT[tc]).inc(ld, 16)
            v.wait(ld)
            s.wait(ld)
            ln_rows(tc, save=False, src=acc[tch])
            for k in range(NDM):
                par = sd_slot % 2
                if sd_slot >= 2:
                    y.wait(ev_v, sd_evv[sd_slot - 2])
                y.dma_start(out=sdl[par][:, 0:TC], in_=s_dram[tc, k]).inc(sdls[par], 16)
                y.dma_start(out=sdl[par][:, TC:2 * TC], in_=d_dram[tc, k]).inc(sdls[par], 16)
                v.wait(sdls[par], sdls[par].n)
                v.scalar_tensor_tensor(ts_slice(hbuf, k, TC), sdl[par][:, TC:2 * TC],
                                       st0[:, k:k + 1], sdl[par][:, 0:TC],
                                       ALU.mult, ALU.add).inc(ev_v)
                sd_evv.append(ev_v.n)
                sd_slot += 1
            g.wait(ev_s, ev_s.n)
            g.wait(mm, dn_mm2)
            for dmo in range(NDM):
                par = wo_slot % 2
                if wo_slot >= 2:
                    g.wait(mm, wo_mm[wo_slot - 2])
                g.dma_start(out=wo[par][:, 0:D], in_=wout_w[dmo]).inc(wol[par], 16)
                t.wait(wol[par], wol[par].n)
                if dmo == 0:
                    t.wait(ev_v)
                pz = [pA, pB][par]
                if wo_slot >= 2:
                    t.wait(ev_v, wo_evv[wo_slot - 2])
                for k in range(NDM):
                    op = t.matmul(pz[:], wo[par][:, k * 128:(k + 1) * 128],
                                  ts_slice(hbuf, k, TC), start=(k == 0), stop=(k == NDM - 1))
                op.inc(mm)
                wo_mm.append(mm.n)
                wo_slot += 1
                v.wait(mm)
                v.tensor_sub(t1f[:], ts_slice(acc[tch], dmo, TC), p_mu[:])
                v.tensor_mul(t1f[:], t1f[:], p_rstd[:])
                v.tensor_add(t1f[:], t1f[:], pz[:])
                v.tensor_add(ts_slice(acc[tch], dmo, TC),
                             ts_slice(acc[tch], dmo, TC), t1f[:])
                v.tensor_copy(z_sl(tch, dmo), t1f[:]).inc(ev_v)
                wo_evv.append(ev_v.n)
        for tch in range(2):
            s.wait(ev_v)
            s.activation(sq[:], scr[:, tch * NDM * TC:(tch + 1) * NDM * TC],
                         AF.Square).inc(ev_s)
            t.wait(ev_s)
            for dmc in range(NDM):
                op = t.matmul(p_s2[0:1, :], C.ones_col[:], ts_slice(sq, dmc, TC),
                              start=(dmc == 0), stop=(dmc == NDM - 1))
            op.inc(mm)
            v.wait(mm)
            v.tensor_scalar(rr_row[:], p_s2[0:1, :], 1.0 / D, 1e-6,
                            ALU.mult, ALU.add).inc(ev_v)
            s.wait(ev_v)
            s.activation(rr_row[:], rr_row[:], AF.Sqrt).inc(ev_s)
            v.wait(ev_s)
            v.reciprocal(rr_row[:], rr_row[:]).inc(ev_v)
            t.wait(ev_v)
            t.matmul(p_rstd[:], C.ones_row_f[:], rr_row[:], start=True, stop=True).inc(mm)
            v.wait(mm)
            for dmc in range(NDM):
                v.tensor_mul(z_sl(tch, dmc), z_sl(tch, dmc), p_rstd[:])
                op = v.tensor_scalar_mul(z_sl(tch, dmc), z_sl(tch, dmc), wffn_c(dmc))
            op.inc(ev_v)
        t.wait(ev_v)
        dn_mm2 = mm.n
        for fg in range(8):
            for fjj in range(8):
                fj = fg * 8 + fjj
                buf = (fg * 8 + fjj) % 2
                if gu_n >= 2:
                    g.wait(ev_v, gu_done[gu_n - 2])
                g.wait(mm, dn_mm2)
                g.dma_start(out=wgu[buf][:, 0:D], in_=wgu_w[fj]).inc(wl[buf], 16)
                g.dma_start(out=wgu[buf][:, D:2 * D], in_=wgu_w[64 + fj]).inc(wl[buf], 16)
                t.wait(wl[buf], wl[buf].n)
                for tch in range(2):
                    par = (fjj * 2 + tch) % 2
                    pg = [pA, p_mu][par]
                    pu = [pB, p_rstd][par]
                    sg2 = [T.b[0], T.b[2]][par]
                    su2 = [T.b[1], T.b[3]][par]
                    u = len(gu_evs)
                    if u >= 2:
                        t.wait(ev_s, gu_evs[u - 2])
                        t.wait(ev_v, gu_evv[u - 2])
                    for dmc in range(NDM):
                        t.matmul(pg[:], wgu[buf][:, dmc * 128:dmc * 128 + 128],
                                 z_sl(tch, dmc), start=(dmc == 0), stop=(dmc == NDM - 1))
                    for dmc in range(NDM):
                        op = t.matmul(pu[:], wgu[buf][:, D + dmc * 128:D + dmc * 128 + 128],
                                      z_sl(tch, dmc), start=(dmc == 0), stop=(dmc == NDM - 1))
                    op.inc(mm)
                    s.wait(mm)
                    if u >= 2:
                        s.wait(ev_v, gu_evv[u - 2])
                    s.activation(sg2[:], pg[:], AF.Silu).inc(ev_s)
                    gu_evs.append(ev_s.n)
                    v.wait(mm)
                    v.tensor_copy(su2[:], pu[:])
                    v.wait(ev_s)
                    v.tensor_mul(act5_sl(fjj, tch), sg2[:], su2[:]).inc(ev_v)
                    gu_evv.append(ev_v.n)
                gu_done.append(ev_v.n)
                gu_n += 1
            g.wait(ev_v, gu_done[-1])
            dwb0 = dnl.n
            for fjj in range(4):
                g.dma_start(out=ts_slice(hbuf, fjj, D),
                            in_=wdn_w[fg * 8 + fjj]).inc(dnl, 16)
            g.wait(ev_s, gu_evs[-1])
            for fjj in range(4, 8):
                g.dma_start(out=T.w4s((fjj - 4) * D, D),
                            in_=wdn_w[fg * 8 + fjj]).inc(dnl, 16)
            t.wait(dnl, dnl.n)
            t.wait(ev_v)
            down_evv = []
            for dmo in range(NDM):
                for tch in range(2):
                    un = dmo * 2 + tch
                    pd = [pA, pB, p_mu, p_rstd][un % 4]
                    if un >= 4:
                        t.wait(ev_v, down_evv[un - 4])
                    for fjj in range(8):
                        lhs = (hbuf[:, fjj * D + dmo * 128:fjj * D + dmo * 128 + 128]
                               if fjj < 4 else
                               T.w4s((fjj - 4) * D + dmo * 128, 128))
                        op = t.matmul(pd[:], lhs, act5_sl(fjj, tch),
                                      start=(fjj == 0), stop=(fjj == 7))
                    op.inc(mm)
                    v.wait(mm)
                    v.tensor_add(ts_slice(acc[tch], dmo, TC),
                                 ts_slice(acc[tch], dmo, TC), pd[:]).inc(ev_v)
                    down_evv.append(ev_v.n)
            dn_mm2 = mm.n
        for tch in range(2):
            tc = half * 2 + tch
            y.wait(ev_v)
            y.dma_start(out=yT[tc], in_=acc[tch][:]).inc(st, 16)
    y.wait(st)
    y.wait(sdst[0], sdst[0].n)
    y.wait(sdst[1], sdst[1].n)


def prep_griffin_weights(inputs, core):
    wa = np.asarray(inputs["L0_attn_w_a"], dtype=np.float32)
    wi = np.asarray(inputs["L0_attn_w_i"], dtype=np.float32)
    wv = np.asarray(inputs["L0_attn_w_v"], dtype=np.float32)
    wout = np.asarray(inputs["L0_attn_out_proj_w"], dtype=np.float32)
    wgu = np.asarray(inputs["L0_attn_w_gate_up"], dtype=np.float32)
    wdn = np.asarray(inputs["L0_attn_w_down"], dtype=np.float32)
    wpre = np.asarray(inputs["L0_attn_pre_norm_w"], dtype=np.float32)
    wffn = np.asarray(inputs["L0_attn_ffn_norm_w"], dtype=np.float32)
    db = np.asarray(inputs["L0_attn_decay_bias"], dtype=np.float32)

    def k_slabs(w):
        return np.ascontiguousarray(
            w.reshape(NDM, 128, NDM, 128).transpose(2, 1, 0, 3).reshape(NDM, 128, NDM * 128))

    wgu_s = np.ascontiguousarray(
        wgu.reshape(NDM, 128, 128, 128).transpose(2, 1, 0, 3).reshape(128, 128, NDM * 128))
    wdn_s = np.ascontiguousarray(wdn.reshape(64, 128, D))
    wcols = np.ascontiguousarray(np.concatenate([
        wpre.reshape(NDM, 128).T, wffn.reshape(NDM, 128).T, db.reshape(NDM, 128).T,
    ], axis=1))
    r = core % 4
    alpha = np.zeros((128, 8), np.float32)
    for rr in range(4):
        a = 1.0 if rr < r else 0.0
        alpha[:, rr] = a
        alpha[:, 4 + rr] = 1.0 - a
    return {"wa_w": k_slabs(wa), "wi_w": k_slabs(wi), "wv_w": k_slabs(wv),
            "wout_w": k_slabs(wout), "wgu_w": wgu_s, "wdn_w": wdn_s,
            "wcols": wcols, "masks": alpha}


def build_griffin_program():
    nc = bass.Bass("TRN2", target_bir_lowering=False)
    S = Sched(nc)
    xT = nc.dram_tensor("xT", [NTC, 128, NDM * TC], F32, kind="ExternalInput").ap()
    wa_w = nc.dram_tensor("wa_w", [NDM, 128, NDM * 128], F32, kind="ExternalInput").ap()
    wi_w = nc.dram_tensor("wi_w", [NDM, 128, NDM * 128], F32, kind="ExternalInput").ap()
    wv_w = nc.dram_tensor("wv_w", [NDM, 128, NDM * 128], F32, kind="ExternalInput").ap()
    wout_w = nc.dram_tensor("wout_w", [NDM, 128, NDM * 128], F32, kind="ExternalInput").ap()
    wgu_w = nc.dram_tensor("wgu_w", [128, 128, NDM * 128], F32, kind="ExternalInput").ap()
    wdn_w = nc.dram_tensor("wdn_w", [64, 128, D], F32, kind="ExternalInput").ap()
    wcols = nc.dram_tensor("wcols", [128, 48], F32, kind="ExternalInput").ap()
    masks = nc.dram_tensor("masks", [128, 8], F32, kind="ExternalInput").ap()
    s_dram = nc.dram_tensor("s_scr", [NTC, NDM, 128, TC], BF16, kind="Internal").ap()
    d_dram = nc.dram_tensor("d_scr", [NTC, NDM, 128, TC], BF16, kind="Internal").ap()
    ag_in = nc.dram_tensor("ag_in", [128, 32], F32, kind="Internal").ap()
    ag_out = nc.dram_tensor("ag_out", [512, 32], F32, kind="Internal").ap()
    consts = declare_consts(nc)
    yT = nc.dram_tensor("yT", [NTC, 128, NDM * TC], F32, kind="ExternalOutput").ap()
    C = Common(S, consts)
    T = SharedTiles(S)
    emit_griffin(S, C, T, xT, yT, wa_w, wi_w, wv_w, wout_w, wgu_w, wdn_w,
                 wcols, masks, s_dram=s_dram, d_dram=d_dram,
                 ag_in=ag_in, ag_out=ag_out)
    S.realize()
    return nc


def run_griffin(x_shards, inputs):
    nc = build_griffin_program()
    cn = host_consts()
    in_maps = []
    for c in range(8):
        w = prep_griffin_weights(inputs, c)
        in_maps.append({"xT": x_shards[c], **w, **cn})
    res = run_bass_kernel_spmd(nc, in_maps, core_ids=list(range(8)))
    return [res.results[c]["yT"] for c in range(8)]




def emit_l1ffn(S, C, T, xT, yT, wgT, a_w, bw_w, scale=0.5):
    g, y, t, s, v = S.g, S.y, S.t, S.s, S.v
    SCALING = 4.0

    x_tc = T.x
    xhi = T.xhi
    sq = T.hbuf
    h_tc = T.hbuf
    wg_sb = T.rw
    mu_row = T.row[0]
    rstd_row = T.row[1]
    tmp = T.f[0]
    exrow = T.e[0]
    prow = T.e[1]
    keepr = T.e[2]
    m1c = T.c[0]
    m2c = T.c[1]
    crow = T.eb0
    cT = T.ct8
    ce_bc = T.ce
    yacc = T.zx

    p_s1 = T.p_s1
    p_s2 = T.p_s2
    p_mu = T.p_mu
    p_rstd = T.p_rstd
    p_ct = T.pD
    p_e = T.pA
    p_h = T.pB

    ld = T.ld
    wld = T.rwl
    mm = T.mm
    ev_v = T.ev_v
    ev_s = T.ev_s
    st = T.st

    def aw_tile(e, dmc):
        i = (e * NDM + dmc) * 64
        return T.scr[:, i:i + 64]


    def hs_tile(e):
        return T.xhi[0:64, e * TC:(e + 1) * TC]

    hsb = T.b[0][0:64, :]

    wld_base = wld.n
    g.dma_start(out=wg_sb[:], in_=wgT[:]).inc(wld, 16)
    g.dma_start(out=T.scr[:, 0:8192].rearrange("p (e f) -> p e f", e=8),
                in_=a_w.rearrange("e p f -> p e f")).inc(wld, 16)

    bwslot = 0
    bw_mm = []
    for tc in range(NTC):
        y.wait(st)
        y.wait(ev_v)
        y.dma_start(out=x_tc[:], in_=xT[tc]).inc(ld, 16)
        v.wait(ld)
        s.wait(ld)

        emit_ln_stats(S, C, x_tc, xhi, sq, p_s1, p_s2, mu_row, rstd_row,
                      p_mu, p_rstd, mm, ev_v, ev_s)
        v.wait(mm)
        for dmc in range(NDM):
            v.tensor_sub(tmp[:], ts_slice(x_tc, dmc, TC), p_mu[:])
            op = v.tensor_mul(ts_slice(h_tc, dmc, TC), tmp[:], p_rstd[:])
        op.inc(ev_v)

        t.wait(ev_v)
        t.wait(wld, wld_base + 32)
        for tt in range(TC // 128):
            for dmc in range(NDM):
                op = t.matmul(p_e[:, 0:8],
                              h_tc[:, dmc * TC + tt * 128: dmc * TC + tt * 128 + 128],
                              ts_slice(wg_sb, dmc, 8),
                              start=(dmc == 0), stop=(dmc == NDM - 1))
            op.inc(mm)
            v.wait(mm)
            s.wait(mm)
            v.reduce_max(m1c[:], p_e[:, 0:8], axis=AX, negate=True).inc(ev_v)
            s.wait(ev_v)
            s.activation(exrow[:], p_e[:, 0:8], AF.Exp, bias=m1c[:]).inc(ev_s)
            v.wait(ev_s)
            v.reduce_sum(m2c[:], exrow[:], axis=AX)
            v.reciprocal(m2c[:], m2c[:])
            v.tensor_scalar_mul(prow[:], exrow[:], m2c[:])
            v.reduce_max(m1c[:], prow[:], axis=AX)
            v.tensor_scalar(keepr[:], prow[:], m1c[:], None, ALU.is_ge)
            v.tensor_scalar(crow[:], keepr[:], m1c[:], float(scale) * SCALING,
                            ALU.mult, ALU.mult).inc(ev_v)
            t.wait(ev_v)
            t.wait(ev_s)
            t.transpose(p_ct[0:8, tt * 128:(tt + 1) * 128], crow[:], C.idn[:]).inc(mm)
        v.wait(mm)
        v.tensor_copy(cT[:, 0:TC], p_ct[0:8, 0:TC]).inc(ev_v)
        t.wait(ev_v)
        for e in range(8):
            t.matmul(p_e[:], C.sel_e(e), cT[:, 0:TC], start=True, stop=True).inc(mm)
            v.wait(mm)
            v.tensor_copy(ts_slice(ce_bc, e, TC), p_e[:]).inc(ev_v)
            t.wait(ev_v)

        hs_evv = []
        for e in range(8):
            ph = [p_h, T.pC][e % 2]
            hsb2 = [T.b[0][0:64, :], T.b[1][0:64, :]][e % 2]
            if e >= 2:
                t.wait(ev_v, hs_evv[e - 2])
            for dmc in range(NDM):
                op = t.matmul(ph[0:64, :], aw_tile(e, dmc),
                              ts_slice(h_tc, dmc, TC),
                              start=(dmc == 0), stop=(dmc == NDM - 1))
            op.inc(mm)
            v.wait(mm)
            v.tensor_copy(hsb2, ph[0:64, :])
            v.tensor_mul(hs_tile(e), hsb2, ce_bc[0:64, e * TC:(e + 1) * TC]) \
                .inc(ev_v)
            hs_evv.append(ev_v.n)

        t.wait(ev_v)
        yo_evv = []
        for dmo in range(NDM):
            par = bwslot % 2
            pe2 = [p_e, T.p_mu][dmo % 2]
            if bwslot >= 2:
                g.wait(mm, bw_mm[bwslot - 2])
            g.dma_start(out=T.sdl[par][0:64, 0:1024], in_=bw_w[dmo]) \
                .inc([T.sl0, T.sl1][par], 16)
            t.wait([T.sl0, T.sl1][par], [T.sl0, T.sl1][par].n)
            if dmo >= 2:
                t.wait(ev_v, yo_evv[dmo - 2])
            for e in range(8):
                op = t.matmul(pe2[:], T.sdl[par][0:64, e * 128:(e + 1) * 128],
                              hs_tile(e), start=(e == 0), stop=(e == 7))
            op.inc(mm)
            bw_mm.append(mm.n)
            bwslot += 1
            v.wait(mm)
            v.tensor_add(ts_slice(yacc, dmo, TC), ts_slice(x_tc, dmo, TC), pe2[:]) \
                .inc(ev_v)
            yo_evv.append(ev_v.n)
        y.wait(ev_v)
        y.dma_start(out=yT[tc], in_=yacc[:]).inc(st, 16)
    y.wait(st)


def prep_l1ffn_weights(inputs):
    A = np.asarray(inputs["L1_ffn_A"], dtype=np.float32)
    Bw = np.asarray(inputs["L1_ffn_Bw"], dtype=np.float32)
    Wg = np.asarray(inputs["L1_ffn_Wg"], dtype=np.float32)
    a_w = np.ascontiguousarray(
        A.transpose(0, 2, 1).reshape(8, NDM, 128, 64).transpose(0, 2, 1, 3)
        .reshape(8, 128, NDM * 64))
    bwt = Bw.transpose(0, 2, 1).reshape(8, 64, NDM, 128)
    bw_w = np.ascontiguousarray(bwt.transpose(2, 1, 0, 3).reshape(NDM, 64, 8 * 128))
    wgT = np.ascontiguousarray(
        Wg.T.reshape(NDM, 128, 8).transpose(1, 0, 2).reshape(128, NDM * 8))
    return {"wgT": wgT, "a_w": a_w, "bw_w": bw_w}


def build_l1ffn_program():
    nc = bass.Bass("TRN2", target_bir_lowering=False)
    S = Sched(nc)
    xT = nc.dram_tensor("xT", [NTC, 128, NDM * TC], F32, kind="ExternalInput").ap()
    wgT = nc.dram_tensor("wgT", [128, NDM * 8], F32, kind="ExternalInput").ap()
    a_w = nc.dram_tensor("a_w", [8, 128, NDM * 64], F32, kind="ExternalInput").ap()
    bw_w = nc.dram_tensor("bw_w", [NDM, 64, 8 * 128], F32, kind="ExternalInput").ap()
    consts = declare_consts(nc)
    yT = nc.dram_tensor("yT", [NTC, 128, NDM * TC], F32, kind="ExternalOutput").ap()
    C = Common(S, consts)
    T = SharedTiles(S)
    emit_l1ffn(S, C, T, xT, yT, wgT, a_w, bw_w, scale=0.5)
    S.realize()
    return nc


def run_l1ffn(x_shards, inputs, scale=0.5):
    nc = build_l1ffn_program()
    cn = host_consts()
    w = prep_l1ffn_weights(inputs)
    in_maps = [{"xT": x_shards[c], **w, **cn} for c in range(8)]
    res = run_bass_kernel_spmd(nc, in_maps, core_ids=list(range(8)))
    return [res.results[c]["yT"] for c in range(8)]

SEQ = 8192
NKT = SEQ // 128
NQC = SEQ // TC


def emit_l1attn(S, C, T, xT, yT, qkv_w, qkvb, woT_w, outb, slopes, t0_base, mmask,
                noff, selmask, hg_in, hg_all, qkv_scr, ag2_in, ag2_out):
    g, y, t, s, v = S.g, S.y, S.t, S.s, S.v
    SCALE = 1.0 / np.sqrt(128.0)

    x_tc = T.x
    xhi = T.xhi
    sq = T.hbuf
    h_tc = T.hbuf
    rhsb = T.xhi
    vt_sb = T.hbuf
    aout = T.hbuf
    qw = [T.ce[:, 0:2048], T.ce[:, 2048:4096]]

    def qt_s(a, b):
        return T.scr[:, a:b]

    def kt_s(a, b):
        return T.scr[:, 8192 + a:8192 + b]

    def vtok_s(ktile):
        return T.w4s(ktile * 128, 128)

    t0h = T.t0h
    offtab = T.offtab
    ssb = T.f[0]
    pt_sb = T.b[0]
    of32 = T.f[1]
    t1f = T.f[2]
    qkvb_sb = T.qkvb
    outb_sb = T.outb
    slop_sb = T.slop
    t0b_sb = T.t0b
    mmask_sb = T.mmask
    noff_sb = T.noff
    selm_sb = T.selm
    mu_row = T.row[0]
    rstd_row = T.row[1]
    m_row = T.r1x1[0]
    m2_row = T.r1x1[1]
    sl_col = T.c[0]
    m_col = T.c[1]
    jcol = T.c[2]
    lrow = T.row[2]

    p_s1 = T.p_s1
    p_s2 = T.p_s2
    p_mu = T.p_mu
    p_rstd = T.p_rstd
    pA = T.pA
    pC = T.pC
    pO = T.pB
    pL = T.p_s1
    pT = T.pD

    ld = T.ld
    cld = T.cld
    mm = T.mm
    ev_v = T.ev_v
    ev_s = T.ev_s
    st = T.st
    qwl = [T.wl0, T.wl1]
    rhl = T.rhl
    qkl = T.qkl
    hst = T.hst
    qst = T.qst
    ast = T.ast
    agc = T.agc

    g.dma_start(out=qkvb_sb[:], in_=qkvb).inc(cld, 16)
    g.dma_start(out=outb_sb[:], in_=outb).inc(cld, 16)
    g.dma_start(out=slop_sb[:], in_=slopes).inc(cld, 16)
    g.dma_start(out=t0b_sb[:], in_=t0_base).inc(cld, 16)
    g.dma_start(out=mmask_sb[:].rearrange("p (j n) -> p j n", j=4),
                in_=mmask.rearrange("j p n -> p j n")).inc(cld, 16)
    g.dma_start(out=noff_sb[:], in_=noff).inc(cld, 16)
    g.dma_start(out=selm_sb[:], in_=selmask).inc(cld, 16)
    v.wait(cld)
    s.wait(cld)
    t.wait(cld)
    v.tensor_scalar_mul(qkvb_sb[:, 0:4], qkvb_sb[:, 0:4], SCALE).inc(ev_v)

    for tc in range(NTC):
        y.wait(hst)
        y.wait(ev_v)
        y.dma_start(out=x_tc[:], in_=xT[tc]).inc(ld, 16)
        v.wait(ld)
        s.wait(ld)
        emit_ln_stats(S, C, x_tc, xhi, sq, p_s1, p_s2, mu_row, rstd_row,
                      p_mu, p_rstd, mm, ev_v, ev_s)
        v.wait(mm)
        for dmc in range(NDM):
            v.tensor_sub(t1f[:], ts_slice(x_tc, dmc, TC), p_mu[:])
            op = v.tensor_mul(ts_slice(h_tc, dmc, TC), t1f[:], p_rstd[:])
        op.inc(ev_v)
        y.wait(ev_v)
        y.dma_start(out=hg_in[tc].rearrange("(c p) n -> p c n", p=128),
                    in_=h_tc[:].rearrange("p (c n) -> p c n", c=NDM)).inc(hst, 16)
    agc_base = agc.n
    for ch_ in range(NTC):
        g.wait(hst, 16 * (ch_ + 1))
        g.collective_compute(
            "AllGather", ALU.bypass,
            replica_groups=[[0, 1, 2, 3], [4, 5, 6, 7]],
            ins=[hg_in[ch_]],
            outs=[hg_all[ch_]],
        ).inc(agc)
    agc_h = agc.n

    qslot = 0
    qs_mm = []
    q_evs = []
    q_evv = []
    q_sts = []
    rhs_mm = 0
    for tb in range(SEQ // TC):
        rr_ = tb // 4
        tl_ = (tb % 4) * TC
        y.wait(agc, agc_base + (tb % 4) + 1)
        y.wait(mm, rhs_mm)
        y.dma_start(out=rhsb[:].rearrange("p (c n) -> p c n", c=NDM),
                    in_=hg_all[tb % 4, rr_ * D:(rr_ + 1) * D, :]
                    .rearrange("(c p) n -> p c n", p=128)).inc(rhl, 16)
        for oc in range(12):
            par = qslot % 2
            sA = [pA, pC][par]
            of2 = [T.f[1], T.f[3]][par]
            pt2 = [T.b[0], T.b[1]][par]
            if qslot >= 2:
                g.wait(mm, qs_mm[qslot - 2])
            g.dma_start(out=qw[par][:], in_=qkv_w[oc]).inc(qwl[par], 16)
            t.wait(qwl[par], qwl[par].n)
            t.wait(rhl, rhl.n)
            if qslot >= 2:
                t.wait(ev_s, q_evs[qslot - 2])
            for dmc in range(NDM):
                op = t.matmul(sA[:], qw[par][:, dmc * 128:dmc * 128 + 128],
                              ts_slice(rhsb, dmc, TC),
                              start=(dmc == 0), stop=(dmc == NDM - 1))
            op.inc(mm)
            qs_mm.append(mm.n)
            s.wait(mm)
            sc = SCALE if oc < 4 else 1.0
            if qslot >= 2:
                s.wait(ev_v, q_evv[qslot - 2])
            s.activation(of2[:], sA[:], AF.Identity, scale=sc).inc(ev_s)
            q_evs.append(ev_s.n)
            v.wait(ev_s)
            if qslot >= 2:
                v.wait(qst, q_sts[qslot - 2])
            v.tensor_scalar_add(pt2[:], of2[:], qkvb_sb[:, oc:oc + 1]).inc(ev_v)
            q_evv.append(ev_v.n)
            y.wait(ev_v)
            y.dma_start(out=qkv_scr[oc, :, tb * TC:(tb + 1) * TC], in_=pt2[:]) \
                .inc(qst, 16)
            q_sts.append(qst.n)
            qslot += 1
        rhs_mm = mm.n

    v.wait(hst, hst.n)
    for hh in range(4):
        y.wait(qst, qst.n)
        y.wait(ast)
        y.dma_start(out=qt_s(0, SEQ), in_=qkv_scr[hh]).inc(qkl, 16)
        y.dma_start(out=kt_s(0, SEQ), in_=qkv_scr[4 + hh]).inc(qkl, 16)
        y.dma_start(out=vt_sb[:], in_=qkv_scr[8 + hh]).inc(qkl, 16)
        t.wait(qkl, qkl.n)
        for ktile in range(NKT):
            t.wait(ev_v)
            t.transpose(pT[:, 0:128], vt_sb[:, ktile * 128:(ktile + 1) * 128],
                        C.idn[:]).inc(mm)
            v.wait(mm)
            v.tensor_copy(vtok_s(ktile), pT[:, 0:128]).inc(ev_v)
        s.wait(qkl, qkl.n)
        s.wait(ev_v, ev_v.n)
        v.memset(m_row[:], -1e30)
        s.activation(sq[:], qt_s(0, SEQ), AF.Square).inc(ev_s)
        t.wait(ev_s)
        for ch in range(NQC):
            t.wait(ev_v)
            t.matmul(p_s1[0:1, :], C.ones_col[:], ts_slice(sq, ch, TC),
                     start=True, stop=True).inc(mm)
            v.wait(mm)
            v.reduce_max(lrow[:, 0:1], p_s1[0:1, :], axis=AX)
            v.tensor_max(m_row[:], m_row[:], lrow[:, 0:1]).inc(ev_v)
        s.wait(ev_v)
        v.memset(m2_row[:], -1e30)
        s.activation(sq[:], kt_s(0, SEQ), AF.Square).inc(ev_s)
        t.wait(ev_s)
        for ch in range(NQC):
            t.wait(ev_v)
            t.matmul(p_s1[0:1, :], C.ones_col[:], ts_slice(sq, ch, TC),
                     start=True, stop=True).inc(mm)
            v.wait(mm)
            v.reduce_max(lrow[:, 0:1], p_s1[0:1, :], axis=AX)
            v.tensor_max(m2_row[:], m2_row[:], lrow[:, 0:1]).inc(ev_v)
        v.tensor_mul(m2_row[:], m2_row[:], m_row[:]).inc(ev_v)
        s.wait(ev_v)
        s.activation(m2_row[:], m2_row[:], AF.Sqrt).inc(ev_s)
        t.wait(ev_s)
        t.matmul(p_mu[:, 0:1], C.ones_row_f[:], m2_row[:], start=True, stop=True)
        t.matmul(p_rstd[:, 0:1], C.ones_row_f[:], slop_sb[:, hh:hh + 1],
                 start=True, stop=True).inc(mm)
        v.wait(mm)
        v.tensor_copy(m_col[:], p_mu[:, 0:1])
        v.tensor_copy(sl_col[:], p_rstd[:, 0:1])
        v.tensor_scalar_mul(t0h[:, 4 * TC:5 * TC], t0b_sb[:], sl_col[:])
        for j in range(4):
            v.tensor_scalar(jcol[:], sl_col[:], float(128 * j), None, ALU.mult)
            v.tensor_sub(jcol[:], jcol[:], m_col[:])
            v.tensor_add(ts_slice(t0h, j, TC), t0h[:, 4 * TC:5 * TC],
                         ts_slice(mmask_sb, j, TC))
            v.tensor_scalar(ts_slice(t0h, j, TC), ts_slice(t0h, j, TC), jcol[:],
                            None, ALU.add)
        v.engine_nop() if False else None
        op = v.tensor_copy(jcol[:], jcol[:])
        op.inc(ev_v)
        t.wait(ev_v)
        t.matmul(p_mu[:, 0:64], C.ones_row_f[:], noff_sb[:], start=True, stop=True) \
            .inc(mm)
        v.wait(mm)
        v.tensor_scalar_mul(offtab[:], p_mu[:, 0:64], sl_col[:])
        v.tensor_scalar(offtab[:], offtab[:], m_col[:], None, ALU.subtract).inc(ev_v)
        t.wait(ev_v)

        sbank = [pA, p_mu]
        ssb2 = [T.f[0], T.f[3]]
        ptb2 = [T.b[0], T.b[1]]
        WIN = int(90.0 * (2 ** (2 * hh + 2)))
        for qc in range(NQC):
            nkt_q = 4 * qc + 4
            kt_lo = max(0, (qc * TC - 127 - WIN) // 128)
            tt_ev = {}
            exp_ev = {}
            pv_mm = {}
            for ktile in range(kt_lo, nkt_q):
                off = ktile - 4 * qc
                i = ktile % 2
                if ktile >= kt_lo + 2:
                    t.wait(ev_v, tt_ev[ktile - 2])
                t.matmul(sbank[i][:], kt_s(ktile * 128, (ktile + 1) * 128),
                         qt_s(qc * TC, (qc + 1) * TC), start=True, stop=True).inc(mm)
                qk_mm = mm.n
                if ktile >= kt_lo + 1:
                    t.wait(ev_s, exp_ev[ktile - 1])
                    t.matmul(pO[:], vtok_s(ktile - 1), ptb2[(ktile - 1) % 2][:],
                             start=(ktile - 1 == kt_lo), stop=False)
                    t.matmul(pL[0:1, :], C.ones_col[:], ptb2[(ktile - 1) % 2][:],
                             start=(ktile - 1 == kt_lo), stop=False).inc(mm)
                    pv_mm[ktile - 1] = mm.n
                v.wait(mm, qk_mm)
                if ktile >= kt_lo + 2:
                    v.wait(ev_s, exp_ev[ktile - 2])
                if off >= 0:
                    v.tensor_add(ssb2[i][:], ts_slice(t0h, off, TC),
                                 sbank[i][:]).inc(ev_v)
                else:
                    v.scalar_tensor_tensor(ssb2[i][:], t0h[:, 4 * TC:5 * TC],
                                           offtab[:, -off:1 - off], sbank[i][:],
                                           ALU.add, ALU.add).inc(ev_v)
                tt_ev[ktile] = ev_v.n
                s.wait(ev_v, tt_ev[ktile])
                if ktile >= kt_lo + 2:
                    s.wait(mm, pv_mm[ktile - 2])
                s.activation(ptb2[i][:], ssb2[i][:], AF.Exp).inc(ev_s)
                exp_ev[ktile] = ev_s.n
            kl = nkt_q - 1
            t.wait(ev_s, exp_ev[kl])
            t.matmul(pO[:], vtok_s(kl), ptb2[kl % 2][:],
                     start=(kl == kt_lo), stop=True)
            op = t.matmul(pL[0:1, :], C.ones_col[:], ptb2[kl % 2][:],
                          start=(kl == kt_lo), stop=True)
            op.inc(mm)
            v.wait(mm)
            v.tensor_copy(lrow[:], pL[0:1, :])
            v.reciprocal(lrow[:], lrow[:]).inc(ev_v)
            t.wait(ev_v)
            t.matmul(p_rstd[:], C.ones_row_f[:], lrow[:], start=True, stop=True).inc(mm)
            s.wait(mm)
            s.copy(of32[:], pO[:]).inc(ev_s)
            v.wait(ev_s)
            v.wait(ast)
            v.tensor_mul(aout[:, qc * TC:(qc + 1) * TC], of32[:], p_rstd[:]).inc(ev_v)
            t.wait(ev_v)
        y.wait(ev_v)
        y.dma_start(out=ag2_in[hh * 128:(hh + 1) * 128, :], in_=aout[:]).inc(ast, 16)

    g.wait(ast, ast.n)
    g.collective_compute(
        "AllGather", ALU.bypass,
        replica_groups=[[0, 1, 2, 3], [4, 5, 6, 7]],
        ins=[ag2_in], outs=[ag2_out],
    ).inc(agc)
    g.wait(agc, agc.n)
    agc_a = agc.n

    woslot = 0
    wo_mm = []
    wo_evs = []
    wo_evv = []
    sel_mm = 0
    y.wait(agc, agc_a)
    for tc in range(NTC):
        y.wait(st)
        y.wait(ev_v)
        y.dma_start(out=x_tc[:], in_=xT[tc]).inc(ld, 16)
        y.wait(mm, sel_mm)
        def blk_slice(gb, dmc):
            if gb == 0:
                return qt_s(dmc * TC, (dmc + 1) * TC)
            if gb == 1:
                return kt_s(dmc * TC, (dmc + 1) * TC)
            if gb == 2:
                return vt_sb[:, dmc * TC:(dmc + 1) * TC]
            return T.w4s(dmc * TC, TC)
        y.dma_start(out=qt_s(0, SEQ).rearrange("p (c n) -> p c n", c=NDM),
                    in_=ag2_out[:, 0 * TOK + tc * TC: 0 * TOK + (tc + 1) * TC]
                    .rearrange("(c p) n -> p c n", p=128)).inc(rhl, 16)
        y.dma_start(out=kt_s(0, SEQ).rearrange("p (c n) -> p c n", c=NDM),
                    in_=ag2_out[:, 1 * TOK + tc * TC: 1 * TOK + (tc + 1) * TC]
                    .rearrange("(c p) n -> p c n", p=128)).inc(rhl, 16)
        y.dma_start(out=vt_sb[:].rearrange("p (c n) -> p c n", c=NDM),
                    in_=ag2_out[:, 2 * TOK + tc * TC: 2 * TOK + (tc + 1) * TC]
                    .rearrange("(c p) n -> p c n", p=128)).inc(rhl, 16)
        y.dma_start(out=T.w4[0][:].rearrange("p (c n) -> p c n", c=8),
                    in_=ag2_out[0:1024, 3 * TOK + tc * TC: 3 * TOK + (tc + 1) * TC]
                    .rearrange("(c p) n -> p c n", p=128)).inc(rhl, 16)
        y.dma_start(out=T.w4[1][:].rearrange("p (c n) -> p c n", c=8),
                    in_=ag2_out[1024:2048, 3 * TOK + tc * TC: 3 * TOK + (tc + 1) * TC]
                    .rearrange("(c p) n -> p c n", p=128)).inc(rhl, 16)
        v.wait(rhl, rhl.n)
        v.wait(ld)
        for dmc in range(NDM):
            v.tensor_scalar_mul(ts_slice(rhsb, dmc, TC),
                                blk_slice(0, dmc), selm_sb[:, 0:1])
        for gb in range(1, 4):
            for dmc in range(NDM):
                v.tensor_scalar(t1f[:], blk_slice(gb, dmc),
                                selm_sb[:, gb:gb + 1], None, ALU.mult)
                v.tensor_add(ts_slice(rhsb, dmc, TC), ts_slice(rhsb, dmc, TC), t1f[:])
        v.tensor_copy(t1f[:], t1f[:]).inc(ev_v)
        t.wait(ev_v)
        for oc in range(NDM):
            par = woslot % 2
            sA = [pA, pC][par]
            of2 = [T.f[1], T.f[3]][par]
            if woslot >= 2:
                g.wait(mm, wo_mm[woslot - 2])
            g.dma_start(out=qw[par][:], in_=woT_w[oc]).inc(qwl[par], 16)
            t.wait(qwl[par], qwl[par].n)
            if woslot >= 2:
                t.wait(ev_s, wo_evs[woslot - 2])
            for dvc in range(NDM):
                op = t.matmul(sA[:], qw[par][:, dvc * 128:dvc * 128 + 128],
                              ts_slice(rhsb, dvc, TC),
                              start=(dvc == 0), stop=(dvc == NDM - 1))
            op.inc(mm)
            wo_mm.append(mm.n)
            woslot += 1
            s.wait(mm)
            if woslot >= 3:
                s.wait(ev_v, wo_evv[woslot - 3])
            s.copy(of2[:], sA[:]).inc(ev_s)
            wo_evs.append(ev_s.n)
            v.wait(ev_s)
            v.tensor_scalar_add(t1f[:], of2[:], outb_sb[:, oc:oc + 1])
            v.tensor_add(ts_slice(x_tc, oc, TC), ts_slice(x_tc, oc, TC), t1f[:]) \
                .inc(ev_v)
            wo_evv.append(ev_v.n)
        sel_mm = mm.n
        y.wait(ev_v)
        y.dma_start(out=yT[tc], in_=x_tc[:]).inc(st, 16)
    y.wait(st)


def prep_l1attn_weights(inputs, core):
    qkv = np.asarray(inputs["L1_attn_qkv_w"], dtype=np.float32)
    qkvb = np.asarray(inputs["L1_attn_qkv_b"], dtype=np.float32)
    outw = np.asarray(inputs["L1_attn_out_w"], dtype=np.float32)
    outb = np.asarray(inputs["L1_attn_out_b"], dtype=np.float32)
    slopes = np.asarray(inputs["L1_attn_alibi_slopes"], dtype=np.float32)
    gme = core % 4
    heads = [4 * hh + gme for hh in range(4)]
    rows = []
    bias = []
    for sec in range(3):
        for h in heads:
            rows.append(qkv[sec * 2048 + h * 128: sec * 2048 + (h + 1) * 128, :])
            bias.append(qkvb[sec * 2048 + h * 128: sec * 2048 + (h + 1) * 128])
    qkv_slice = np.concatenate(rows, axis=0)
    qkvT = qkv_slice.T
    qkv_s = np.ascontiguousarray(
        qkvT.reshape(NDM, 128, 12, 128).transpose(2, 1, 0, 3).reshape(12, 128, NDM * 128))
    qkvb_cols = np.ascontiguousarray(np.stack(bias, axis=1))
    woT = outw.T
    perm = np.zeros(D, dtype=np.int64)
    for r in range(4):
        for hh in range(4):
            h = 4 * hh + r
            perm[r * 512 + hh * 128: r * 512 + (hh + 1) * 128] =                 np.arange(h * 128, (h + 1) * 128)
    woT = woT[perm]
    wo_s = np.ascontiguousarray(
        woT.reshape(NDM, 128, NDM, 128).transpose(2, 1, 0, 3).reshape(NDM, 128, NDM * 128))
    outb_cols = np.ascontiguousarray(outb.reshape(NDM, 128).T)
    slop = np.ascontiguousarray(slopes[heads][None, :])
    kk = np.arange(128, dtype=np.float32)[:, None]
    qq = np.arange(TC, dtype=np.float32)[None, :]
    t0 = kk - qq
    mmask = np.zeros((4, 128, TC), np.float32)
    for j in range(4):
        mmask[j][(kk + 128 * j) > qq] = -1e9
    noff = (-128.0 * np.arange(64, dtype=np.float32))[None, :]
    selm = np.zeros((128, 4), np.float32)
    selm[:, gme] = 1.0
    return {"qkv_w": qkv_s, "qkvb": qkvb_cols, "woT_w": wo_s, "outb": outb_cols,
            "slopes": slop, "t0_base": np.ascontiguousarray(t0),
            "mmask": mmask, "noff": noff, "selmask": selm}


def build_l1attn_program():
    nc = bass.Bass("TRN2", target_bir_lowering=False)
    S = Sched(nc)
    xT = nc.dram_tensor("xT", [NTC, 128, NDM * TC], F32, kind="ExternalInput").ap()
    qkv_w = nc.dram_tensor("qkv_w", [12, 128, NDM * 128], F32, kind="ExternalInput").ap()
    qkvb = nc.dram_tensor("qkvb", [128, 12], F32, kind="ExternalInput").ap()
    woT_w = nc.dram_tensor("woT_w", [NDM, 128, NDM * 128], F32, kind="ExternalInput").ap()
    outb = nc.dram_tensor("outb", [128, 16], F32, kind="ExternalInput").ap()
    slopes = nc.dram_tensor("slopes", [1, 4], F32, kind="ExternalInput").ap()
    t0_base = nc.dram_tensor("t0_base", [128, TC], F32, kind="ExternalInput").ap()
    mmask = nc.dram_tensor("mmask", [4, 128, TC], F32, kind="ExternalInput").ap()
    noff = nc.dram_tensor("noff", [1, 64], F32, kind="ExternalInput").ap()
    selmask = nc.dram_tensor("selmask", [128, 4], F32, kind="ExternalInput").ap()
    hg_in = nc.dram_tensor("hg_in", [NTC, D, TC], BF16, kind="Internal").ap()
    hg_all = nc.dram_tensor("hg_all", [NTC, 4 * D, TC], BF16, kind="Internal").ap()
    qkv_scr = nc.dram_tensor("qkv_scr", [12, 128, SEQ], BF16, kind="Internal").ap()
    ag2_in = nc.dram_tensor("ag2_in", [512, SEQ], BF16, kind="Internal").ap()
    ag2_out = nc.dram_tensor("ag2_out", [D, SEQ], BF16, kind="Internal").ap()
    consts = declare_consts(nc)
    yT = nc.dram_tensor("yT", [NTC, 128, NDM * TC], F32, kind="ExternalOutput").ap()
    C = Common(S, consts)
    T = SharedTiles(S)
    emit_l1attn(S, C, T, xT, yT, qkv_w, qkvb, woT_w, outb, slopes, t0_base, mmask,
                noff, selmask, hg_in, hg_all, qkv_scr, ag2_in, ag2_out)
    S.realize()
    return nc


def run_l1attn(x_shards, inputs):
    nc = build_l1attn_program()
    cn = host_consts()
    in_maps = []
    for c in range(8):
        w = prep_l1attn_weights(inputs, c)
        in_maps.append({"xT": x_shards[c], **w, **cn})
    res = run_bass_kernel_spmd(nc, in_maps, core_ids=list(range(8)))
    return [res.results[c]["yT"] for c in range(8)]





def build_moe_program():
    nc = bass.Bass("TRN2", target_bir_lowering=False)
    S = Sched(nc)
    xT = nc.dram_tensor("xT", [NTC, 128, NDM * TC], F32, kind="ExternalInput").ap()
    rw = nc.dram_tensor("rw", [128, NDM * 8], F32, kind="ExternalInput").ap()
    gate_w = nc.dram_tensor("gate_w", [8, 8, 128, NDM * 128], F32, kind="ExternalInput").ap()
    up_w = nc.dram_tensor("up_w", [8, 8, 128, NDM * 128], F32, kind="ExternalInput").ap()
    down_w = nc.dram_tensor("down_w", [8, 1024, D], F32, kind="ExternalInput").ap()
    consts = declare_consts(nc)
    yT = nc.dram_tensor("yT", [NTC, 128, NDM * TC], F32, kind="ExternalOutput").ap()
    C = Common(S, consts)
    T = SharedTiles(S)
    emit_moe(S, C, T, xT, yT, rw, gate_w, up_w, down_w, scale=0.5)
    S.realize()
    return nc



def shard_xT(x):
    xf = x.reshape(2 * 8192, 2048).T
    out = []
    for c in range(8):
        sh = xf[:, c * TOK:(c + 1) * TOK]
        t = sh.reshape(NDM, 128, NTC, TC).transpose(2, 1, 0, 3).reshape(NTC, 128, NDM * TC)
        out.append(np.ascontiguousarray(t))
    return out


def unshard_xT(shards):
    cols = []
    for c in range(8):
        t = shards[c].reshape(NTC, 128, NDM, TC).transpose(2, 1, 0, 3).reshape(D, TOK)
        cols.append(t)
    xf = np.concatenate(cols, axis=1)
    return np.ascontiguousarray(xf.T).reshape(2, 8192, 2048)


def prep_moe_weights(inputs):
    gate = np.asarray(inputs["L0_ffn_gate_proj"], dtype=np.float32)
    up = np.asarray(inputs["L0_ffn_up_proj"], dtype=np.float32)
    down = np.asarray(inputs["L0_ffn_down_proj"], dtype=np.float32)
    rw = np.asarray(inputs["L0_ffn_router_w"], dtype=np.float32)
    def arr_gu(w):
        return np.ascontiguousarray(
            w.reshape(8, NDM, 128, 8, 128).transpose(0, 3, 2, 1, 4)
            .reshape(8, 8, 128, NDM * 128))
    rw2 = np.ascontiguousarray(rw.reshape(NDM, 128, 8).transpose(1, 0, 2).reshape(128, NDM * 8))
    return {"gate_w": arr_gu(gate), "up_w": arr_gu(up),
            "down_w": np.ascontiguousarray(down), "rw": rw2}


def run_moe(x_shards, inputs, scale=0.5):
    nc = build_moe_program()
    cn = host_consts()
    w = prep_moe_weights(inputs)
    in_maps = []
    for c in range(8):
        in_maps.append({"xT": x_shards[c], **w, **cn})
    res = run_bass_kernel_spmd(nc, in_maps, core_ids=list(range(8)))
    return [res.results[c]["yT"] for c in range(8)]





def build_full_program():
    nc = bass.Bass("TRN2", target_bir_lowering=False)
    S = Sched(nc)
    xT = nc.dram_tensor("xT", [NTC, 128, NDM * TC], F32, kind="ExternalInput").ap()
    rw = nc.dram_tensor("rw", [128, NDM * 8], F32, kind="ExternalInput").ap()
    gate_w = nc.dram_tensor("gate_w", [8, 8, 128, NDM * 128], F32, kind="ExternalInput").ap()
    up_w = nc.dram_tensor("up_w", [8, 8, 128, NDM * 128], F32, kind="ExternalInput").ap()
    down_w = nc.dram_tensor("down_w", [8, 1024, D], F32, kind="ExternalInput").ap()
    wa_w = nc.dram_tensor("wa_w", [NDM, 128, NDM * 128], F32, kind="ExternalInput").ap()
    wi_w = nc.dram_tensor("wi_w", [NDM, 128, NDM * 128], F32, kind="ExternalInput").ap()
    wv_w = nc.dram_tensor("wv_w", [NDM, 128, NDM * 128], F32, kind="ExternalInput").ap()
    wout_w = nc.dram_tensor("wout_w", [NDM, 128, NDM * 128], F32, kind="ExternalInput").ap()
    wgu_w = nc.dram_tensor("wgu_w", [128, 128, NDM * 128], F32, kind="ExternalInput").ap()
    wdn_w = nc.dram_tensor("wdn_w", [64, 128, D], F32, kind="ExternalInput").ap()
    wcols = nc.dram_tensor("wcols", [128, 48], F32, kind="ExternalInput").ap()
    masks = nc.dram_tensor("masks", [128, 8], F32, kind="ExternalInput").ap()
    wgT = nc.dram_tensor("wgT", [128, NDM * 8], F32, kind="ExternalInput").ap()
    a_w = nc.dram_tensor("a_w", [8, 128, NDM * 64], F32, kind="ExternalInput").ap()
    bw_w = nc.dram_tensor("bw_w", [NDM, 64, 8 * 128], F32, kind="ExternalInput").ap()
    qkv_w = nc.dram_tensor("qkv_w", [12, 128, NDM * 128], F32, kind="ExternalInput").ap()
    qkvb = nc.dram_tensor("qkvb", [128, 12], F32, kind="ExternalInput").ap()
    woT_w = nc.dram_tensor("woT_w", [NDM, 128, NDM * 128], F32, kind="ExternalInput").ap()
    outb = nc.dram_tensor("outb", [128, 16], F32, kind="ExternalInput").ap()
    slopes = nc.dram_tensor("slopes", [1, 4], F32, kind="ExternalInput").ap()
    t0_base = nc.dram_tensor("t0_base", [128, TC], F32, kind="ExternalInput").ap()
    mmask = nc.dram_tensor("mmask", [4, 128, TC], F32, kind="ExternalInput").ap()
    noff = nc.dram_tensor("noff", [1, 64], F32, kind="ExternalInput").ap()
    selmask = nc.dram_tensor("selmask", [128, 4], F32, kind="ExternalInput").ap()
    consts = declare_consts(nc)
    s_dram = nc.dram_tensor("s_scr", [NTC, NDM, 128, TC], BF16, kind="Internal").ap()
    d_dram = nc.dram_tensor("d_scr", [NTC, NDM, 128, TC], BF16, kind="Internal").ap()
    ag_in = nc.dram_tensor("ag_in", [128, 32], F32, kind="Internal").ap()
    ag_out = nc.dram_tensor("ag_out", [512, 32], F32, kind="Internal").ap()
    hg_in = nc.dram_tensor("hg_in", [NTC, D, TC], BF16, kind="Internal").ap()
    hg_all = nc.dram_tensor("hg_all", [NTC, 4 * D, TC], BF16, kind="Internal").ap()
    qkv_scr = nc.dram_tensor("qkv_scr", [12, 128, SEQ], BF16, kind="Internal").ap()
    ag2_in = nc.dram_tensor("ag2_in", [512, SEQ], BF16, kind="Internal").ap()
    ag2_out = nc.dram_tensor("ag2_out", [D, SEQ], BF16, kind="Internal").ap()
    xA = nc.dram_tensor("xA", [NTC, 128, NDM * TC], F32, kind="Internal").ap()
    xB = nc.dram_tensor("xB", [NTC, 128, NDM * TC], F32, kind="Internal").ap()
    yT = nc.dram_tensor("yT", [NTC, 128, NDM * TC], F32, kind="ExternalOutput").ap()

    C = Common(S, consts)
    T = SharedTiles(S)
    emit_moe(S, C, T, xT, xA, rw, gate_w, up_w, down_w, scale=0.5)
    S.barrier()
    emit_griffin(S, C, T, xA, xB, wa_w, wi_w, wv_w, wout_w, wgu_w, wdn_w,
                 wcols, masks, s_dram=s_dram, d_dram=d_dram,
                 ag_in=ag_in, ag_out=ag_out)
    S.barrier()
    emit_moe(S, C, T, xB, xA, rw, gate_w, up_w, down_w, scale=0.5)
    S.barrier()
    emit_l1ffn(S, C, T, xA, xB, wgT, a_w, bw_w, scale=0.5)
    S.barrier()
    emit_l1attn(S, C, T, xB, xA, qkv_w, qkvb, woT_w, outb, slopes, t0_base,
                mmask, noff, selmask, hg_in, hg_all, qkv_scr, ag2_in, ag2_out)
    S.barrier()
    emit_l1ffn(S, C, T, xA, yT, wgT, a_w, bw_w, scale=0.5)
    S.realize()
    return S, nc


def kernel(**inputs):
    x = np.asarray(inputs["L0_attn_x"], dtype=np.float32)
    shards = shard_xT(x)
    S, nc = build_full_program()
    cn = host_consts()
    wm = prep_moe_weights(inputs)
    wf = prep_l1ffn_weights(inputs)
    in_maps = []
    for c in range(8):
        wg = prep_griffin_weights(inputs, c)
        wa = prep_l1attn_weights(inputs, c)
        in_maps.append({"xT": shards[c], **wm, **wg, **wf, **wa, **cn})
    res = run_bass_kernel_spmd(nc, in_maps, core_ids=list(range(8)))
    return unshard_xT([res.results[c]["yT"] for c in range(8)])
